# version 02
# baseline (speedup 1.0000x reference)
import numpy as np
import ml_dtypes
from contextlib import ExitStack

import concourse.bass as bass
import concourse.mybir as mybir
from concourse.bass_utils import run_bass_kernel_spmd

F32 = mybir.dt.float32
BF16 = mybir.dt.bfloat16
AF = mybir.ActivationFunctionType
AL = mybir.AluOpType
AP = bass.AP

NCORES = 8
B_PER = 4
S = 2048
D = 1024
NDC = D // 128
H = 16
HD = 64
SJ = 512
NJ = S // SJ
NKT = S // 128
POS = S // NCORES
T = 32

LN_EPS = 1e-5
EXP_BIAS = -4.0

ENGINES = ("sync", "gpsimd", "tensor", "vector", "scalar")


class Plan:
    def __init__(self, nc, ctx):
        self.nc = nc
        self.ctx = ctx
        self.ops = {e: [] for e in ENGINES}
        self.sems = {}
        self.counts = {}
        for name in ("pe", "act", "dve", "pl", "cc"):
            self.get_sem(name)

    def get_sem(self, name):
        if name not in self.sems:
            self.sems[name] = self.ctx.enter_context(self.nc.semaphore(name))
            self.counts[name] = 0
        return self.sems[name]

    def emit(self, eng, fn, waits=(), inc=None, inc_n=1):
        waits = tuple(w for w in waits if w is not None)
        incs = []
        if inc is not None:
            self.counts[inc] += inc_n
            incs.append((inc, self.counts[inc], inc_n))
        self.ops[eng].append((waits, fn, tuple(incs)))
        return (inc, self.counts[inc]) if inc else None

    def dma(self, eng, out, in_, sem, waits=(), **kw):
        self.get_sem(sem)
        return self.emit(eng, lambda e: e.dma_start(out=out, in_=in_, **kw),
                         waits, inc=sem, inc_n=16)

    def mm(self, out, lhsT, rhs, start, stop, waits=(), tp=None, inc=False,
           sgc=False):
        kw = {}
        if tp is not None:
            kw["tile_position"] = tp
        if sgc:
            kw["skip_group_check"] = True
        return self.emit(
            "tensor",
            lambda e: e.matmul(out, lhsT, rhs, start=start, stop=stop, **kw),
            waits, inc=("pe" if inc else None))

    def act(self, out, in_, func, waits=(), bias=0.0, scale=1.0, accum=None):
        kw = {"bias": bias, "scale": scale}
        if accum is not None:
            kw["accum_out"] = accum
        return self.emit("scalar",
                         lambda e: e.activation(out, in_, func, **kw),
                         waits, inc="act")

    def dve(self, fname, *args, waits=(), **kw):
        return self.emit("vector", lambda e: getattr(e, fname)(*args, **kw),
                         waits, inc="dve")

    def pool(self, fname, *args, waits=(), **kw):
        return self.emit("gpsimd", lambda e: getattr(e, fname)(*args, **kw),
                         waits, inc="pl", inc_n=1)

    def replay(self, block):
        own = {"vector": "dve", "scalar": "act"}
        def runner(engname):
            def f(eng):
                last = {}
                i = 0
                osem = own.get(engname)
                for waits, fn, incs in self.ops[engname]:
                    if osem is not None and i > 0:
                        ws = list(waits) + [(osem, i)]
                    else:
                        ws = waits
                    for s, v in ws:
                        if last.get(s, 0) < v:
                            eng.wait_ge(self.sems[s], v)
                            last[s] = v
                    inst = fn(eng)
                    for s, v, n in incs:
                        inst.then_inc(self.sems[s], n)
                    i += 1
            return f
        block.sync(runner("sync"))
        block.gpsimd(runner("gpsimd"))
        block.tensor(runner("tensor"))
        block.vector(runner("vector"))
        block.scalar(runner("scalar"))


def sbt(nc, ctx, name, shape, dt):
    return ctx.enter_context(nc.sbuf_tensor(name, shape, dt))


def pst(nc, ctx, name, shape, dt=F32):
    return ctx.enter_context(nc.psum_tensor(name, shape, dt))


def make_ln(nc, ctx, p, ntok, ps_stat, ps_bc):
    xbf = sbt(nc, ctx, "ln_xbf", [128, NDC * ntok], BF16)
    xsq = sbt(nc, ctx, "ln_xsq", [128, 2 * ntok], BF16)
    rows = sbt(nc, ctx, "ln_rows", [128, 3 * ntok], F32)
    mrbc = sbt(nc, ctx, "ln_mrbc", [128, ntok], F32)
    ones_bf = sbt(nc, ctx, "ln_ones_bf", [128, 1], BF16)
    ones1f = sbt(nc, ctx, "ln_ones1f", [1, 128], F32)
    eps_sb = sbt(nc, ctx, "ln_eps", [1, 1], F32)
    p.pool("memset", ones_bf[:], 1.0)
    p.pool("memset", ones1f[:], 1.0)
    p.pool("memset", eps_sb[:], LN_EPS)

    def ln(xrows, y_view, wait_loads=(), prev=None):
        prev = prev or {}
        t_x = []
        stat_last = None
        sq_mm = [None, None]
        for dc in range(NDC):
            o = dc * ntok
            sq = dc % 2
            tb = p.dma("gpsimd", xbf[:, o:o + ntok], xrows(dc),
                       sem="lnx%d" % dc,
                       waits=list(wait_loads) + [prev.get("stats"),
                                                 prev.get("apply")])
            t_x.append(tb)
            p.mm(ps_stat[0:1, 0:ntok], ones_bf[:, 0:1], xbf[:, o:o + ntok],
                 start=(dc == 0), stop=(dc == NDC - 1),
                 waits=[tb] + ([prev.get("rows_dve")] if dc == 0 else []),
                 sgc=True)
            ts_ = p.act(xsq[:, sq * ntok:(sq + 1) * ntok], xbf[:, o:o + ntok],
                        AF.Square, waits=[tb, sq_mm[sq]])
            stat_last = p.mm(ps_stat[32:33, 0:ntok], ones_bf[:, 0:1],
                             xsq[:, sq * ntok:(sq + 1) * ntok],
                             start=(dc == 0), stop=(dc == NDC - 1),
                             waits=[ts_], inc=True, sgc=True)
            sq_mm[sq] = stat_last
        m_row = rows[64:65, 0:ntok]
        m2_row = rows[64:65, ntok:2 * ntok]
        var_row = rows[0:1, 2 * ntok:3 * ntok]
        r_row = rows[0:1, 0:ntok]
        mr_row = rows[0:1, ntok:2 * ntok]
        p.dve("tensor_scalar_mul", m_row, ps_stat[0:1, 0:ntok], 1.0 / D,
              waits=[stat_last])
        p.dve("tensor_mul", m2_row, m_row, m_row)
        tv = p.dve("scalar_tensor_tensor", var_row, ps_stat[32:33, 0:ntok],
                   1.0 / D, m2_row, AL.mult, AL.subtract)
        tstd = p.act(var_row, var_row, AF.Sqrt, bias=eps_sb[:], waits=[tv])
        tr = p.dve("reciprocal", r_row, var_row, waits=[tstd])
        tmr = p.dve("scalar_tensor_tensor", mr_row, ps_stat[0:1, 0:ntok],
                    1.0 / D, r_row, AL.mult, AL.mult)
        trb = p.mm(ps_bc[:, 0:ntok], ones1f[0:1, :], r_row, start=True,
                   stop=True, waits=[tr, prev.get("apply")], inc=True)
        tmb = p.mm(ps_bc[:, ntok:2 * ntok], ones1f[0:1, :], mr_row, start=True,
                   stop=True, waits=[tmr, prev.get("mrbc_dve")], inc=True)
        tme = p.dve("tensor_copy", mrbc[:], ps_bc[:, ntok:2 * ntok],
                    waits=[tmb])
        ty = None
        for dc in range(NDC):
            o = dc * ntok
            p.dve("tensor_mul", y_view(dc), xbf[:, o:o + ntok],
                  ps_bc[:, 0:ntok], waits=[t_x[dc], trb])
            ty = p.dve("tensor_sub", y_view(dc), y_view(dc), mrbc[:])
        return {"apply": ty, "stats": stat_last, "rows_dve": tv,
                "mrbc_dve": tme, "x_loads": t_x}
    return ln


def build_attn_layer(rope, nb=B_PER):
    nc = bass.Bass("TRN2", target_bir_lowering=False)
    x_d = nc.dram_tensor("x", [nb, D, S], F32, kind="ExternalInput").ap()
    if rope:
        wqkv_d = nc.dram_tensor("wqkvT", [D, 3 * D], F32, kind="ExternalInput").ap()
    else:
        wq_d = nc.dram_tensor("Wq", [D, D], F32, kind="ExternalInput").ap()
        wk_d = nc.dram_tensor("Wk", [D, D], F32, kind="ExternalInput").ap()
        wv_d = nc.dram_tensor("Wv", [D, D], F32, kind="ExternalInput").ap()
    wout_d = nc.dram_tensor("woutT", [D, D], F32, kind="ExternalInput").ap()
    bout_d = nc.dram_tensor("bout_rs", [128, NDC], F32, kind="ExternalInput").ap()
    if rope:
        cos_d = nc.dram_tensor("cos_rep", [128, S], F32, kind="ExternalInput").ap()
        sin_d = nc.dram_tensor("sin_rep", [128, S], F32, kind="ExternalInput").ap()
    x1_d = nc.dram_tensor("x1", [nb, D, S], F32, kind="ExternalOutput").ap()

    with ExitStack() as ctx:
        wbuf = sbt(nc, ctx, "wbuf", [128, 8 * D], BF16)
        y_sb = sbt(nc, ctx, "y_sb", [128, NDC * S], BF16)
        q_sb = sbt(nc, ctx, "q_sb", [128, NDC * S], BF16)
        k_sb = sbt(nc, ctx, "k_sb", [128, NDC * S], BF16)
        VST = 1536
        v_sb = sbt(nc, ctx, "v_sb", [128, NKT * VST], BF16)
        av_sb = sbt(nc, ctx, "av_sb", [128, NDC * SJ], BF16)
        if rope:
            cos_sb = sbt(nc, ctx, "cos_sb", [128, S], BF16)
            sin_sb = sbt(nc, ctx, "sin_sb", [128, S], BF16)
        recip_sb = sbt(nc, ctx, "recip_sb", [1, SJ], F32)
        rb_sb = sbt(nc, ctx, "rb_sb", [64, SJ], F32)
        ones64f = sbt(nc, ctx, "ones64f", [1, 64], F32)
        bout_sb = sbt(nc, ctx, "bout_sb", [128, NDC], F32)
        tmp_f = [sbt(nc, ctx, f"tmpf{i}", [128, SJ], F32) for i in range(2)]
        x1t = [sbt(nc, ctx, f"x1t{i}", [128, SJ], F32) for i in range(2)]

        ebias_sb = sbt(nc, ctx, "ebias", [128, 1], F32)
        psX = [pst(nc, ctx, f"ps{i}", [128, 1024]) for i in range(4)]

        p = Plan(nc, ctx)
        ln = make_ln(nc, ctx, p, SJ, ps_stat=psX[2][0:64, 0:512],
                     ps_bc=psX[3][:, 0:1024])

        p.pool("memset", ones64f[:], 1.0)
        p.pool("memset", ebias_sb[:], EXP_BIAS)
        t_ones = None
        for pr in range(8):
            ones_ap = AP(v_sb, pr * 192 + 64,
                         [[NKT * VST, 128], [VST, NKT], [1, 64]])
            t_ones = p.pool("memset", ones_ap, 1.0)
        p.dma("gpsimd", bout_sb[:], bout_d[:, :], sem="cst")
        if rope:
            p.dma("gpsimd", cos_sb[:], cos_d[:, :], sem="cst")
            t_cst = p.dma("gpsimd", sin_sb[:], sin_d[:, :], sem="cst")
        else:
            t_cst = ("cst", p.counts["cst"])
        t_bout = t_cos = t_sin = t_cst

        stage_free = None
        qk_free = None
        v_free = None
        w_free = None
        av_free = None
        ln_prev = None
        norm_done = [None] * 4
        outproj_act = [None, None]
        outproj_dve = [None, None]
        outproj_st = [None, None]
        outproj_ld = [None, None]

        def vaug(sc, h):
            off = sc * VST + (h // 2) * 192 + (64 if h % 2 else 0)
            return AP(v_sb, off, [[NKT * VST, 128], [1, 128]])

        for b in range(nb):
            y_tok = None
            for j in range(NJ):
                def xrows(dc, b=b, j=j):
                    return x_d[b, dc * 128:(dc + 1) * 128, j * SJ:(j + 1) * SJ]
                def yv(dc, b=b, j=j):
                    return y_sb[:, dc * S + j * SJ:dc * S + (j + 1) * SJ]
                ln_prev = ln(xrows, yv,
                             wait_loads=[stage_free, av_free,
                                         norm_done[2], norm_done[3]],
                             prev=ln_prev)
                y_tok = ln_prev["apply"]

            def load_wblock(key, waits):
                tok = None
                for dc in range(NDC):
                    if rope and key < 3:
                        src = wqkv_d[dc * 128:(dc + 1) * 128,
                                     key * D:(key + 1) * D]
                    elif key == 3:
                        src = wout_d[dc * 128:(dc + 1) * 128, :]
                    else:
                        src = {0: wq_d, 1: wk_d, 2: wv_d}[key]
                        src = src[dc * 128:(dc + 1) * 128, :]
                    tok = p.dma("gpsimd", wbuf[:, dc * D:(dc + 1) * D], src,
                                sem="wld", waits=waits)
                return tok

            if not rope:
                pass
            tw = load_wblock(0, [w_free])
            slot_evac = [None] * 4
            nps = 0

            def qk_block(dst_sb, tw, first_waits):
                nonlocal nps
                tok = None
                for ec in range(NDC):
                    for j in range(NJ):
                        sl = nps % 4
                        ps = psX[sl % 2][:, (sl // 2) * 512:(sl // 2) * 512 + 512]
                        nps += 1
                        for dc in range(NDC):
                            w0 = []
                            if dc == 0:
                                w0 = [tw, slot_evac[sl]] + list(first_waits)
                            t = p.mm(ps, wbuf[:, dc * D + ec * 128:dc * D + (ec + 1) * 128],
                                     y_sb[:, dc * S + j * SJ:dc * S + (j + 1) * SJ],
                                     start=(dc == 0), stop=(dc == NDC - 1),
                                     waits=w0, inc=(dc == NDC - 1))
                        tok = p.dve("tensor_copy",
                                    dst_sb[:, ec * S + j * SJ:ec * S + (j + 1) * SJ],
                                    ps, waits=[t])
                        slot_evac[sl] = tok
                return tok

            tq = qk_block(q_sb, tw, [y_tok, qk_free])
            tw = load_wblock(1, [tq])
            tk = qk_block(k_sb, tw, [])
            tw = load_wblock(2, [tk])
            tv_tok = None
            for sc in range(NKT):
                for dv2 in range(2):
                    sl = nps % 4
                    ps = psX[sl % 2][:, (sl // 2) * 512:(sl // 2) * 512 + 512]
                    nps += 1
                    for dc in range(NDC):
                        w0 = []
                        if dc == 0:
                            w0 = [tw, slot_evac[sl], v_free]
                        t = p.mm(ps, y_sb[:, dc * S + sc * 128:dc * S + (sc + 1) * 128],
                                 wbuf[:, dc * D + dv2 * 512:dc * D + (dv2 + 1) * 512],
                                 start=(dc == 0), stop=(dc == NDC - 1),
                                 waits=w0, inc=(dc == NDC - 1))
                    dst0 = sc * VST + 4 * dv2 * 192
                    p.dve("tensor_copy",
                          AP(v_sb, dst0, [[NKT * VST, 128], [192, 4], [1, 64]]),
                          AP(ps.tensor, ps.offset, [[1024, 128], [128, 4], [1, 64]]),
                          waits=[t, t_ones])
                    tv_tok = p.dve(
                        "tensor_copy",
                        AP(v_sb, dst0 + 128, [[NKT * VST, 128], [192, 4], [1, 64]]),
                        AP(ps.tensor, ps.offset + 64, [[1024, 128], [128, 4], [1, 64]]),
                        waits=[t])
                    slot_evac[sl] = tv_tok
            tw_out = load_wblock(3, [tv_tok])

            if rope:
                last = None
                for src_sb, twait in ((q_sb, tq), (k_sb, tk)):
                    for c in range(4):
                        QE = src_sb[:, c * S:(c + 1) * S]
                        QO = src_sb[:, (c + 4) * S:(c + 5) * S]
                        s0, s1 = av_sb[:, 0:S], av_sb[:, S:2 * S]
                        p.dve("tensor_mul", s0, QE, cos_sb[:],
                              waits=[twait, t_cos, t_sin, qk_free, av_free])
                        p.dve("tensor_mul", s1, QE, sin_sb[:])
                        p.dve("tensor_mul", QE, QO, sin_sb[:])
                        p.dve("tensor_sub", QE, s0, QE)
                        p.dve("tensor_mul", QO, QO, cos_sb[:])
                        last = p.dve("tensor_add", QO, s1, QO)
                qk_ready = last
            else:
                qk_ready = tv_tok

            stage = y_sb
            av_written = None
            for j in range(NJ):
                for g in range(4):
                    heads = [4 * g + m for m in range(4)]

                    def score_mm(m, kt, half, w0):
                        ps = psX[m][:, half * 512:half * 512 + 512]
                        if rope:
                            p.mm(ps, k_sb[32 * m:32 * m + 32,
                                          g * S + kt * 128:g * S + (kt + 1) * 128],
                                 q_sb[32 * m:32 * m + 32,
                                      g * S + j * SJ:g * S + (j + 1) * SJ],
                                 start=True, stop=False, tp=(32 * m, 0),
                                 waits=w0, sgc=True)
                            return p.mm(ps, k_sb[32 * m:32 * m + 32,
                                                 (g + 4) * S + kt * 128:(g + 4) * S + (kt + 1) * 128],
                                        q_sb[32 * m:32 * m + 32,
                                             (g + 4) * S + j * SJ:(g + 4) * S + (j + 1) * SJ],
                                        start=False, stop=True, tp=(32 * m, 0),
                                        inc=True, sgc=True)
                        else:
                            ch = 2 * g + m // 2
                            r0 = 64 * (m % 2)
                            return p.mm(ps, k_sb[r0:r0 + 64,
                                                 ch * S + kt * 128:ch * S + (kt + 1) * 128],
                                        q_sb[r0:r0 + 64,
                                             ch * S + j * SJ:ch * S + (j + 1) * SJ],
                                        start=True, stop=True, tp=(r0, 0),
                                        waits=w0, inc=True, sgc=True)

                    exp_tok = {}
                    prev_exp = None
                    first = (b == 0 and j == 0 and g == 0)
                    for ktp in range(4):
                        sc_toks = []
                        for m in range(4):
                            for kk in range(2):
                                w0 = []
                                if ktp == 0 and kk == 0:
                                    w0 = [qk_ready, norm_done[m]]
                                    if m < 2:
                                        w0.append(outproj_act[m])
                                    if first:
                                        w0.append(tv_tok)
                                elif kk == 0 and m == 0 and prev_exp is not None:
                                    w0 = [prev_exp]
                                t = score_mm(m, 2 * ktp + kk, kk, w0)
                            sc_toks.append(t)
                        for m in range(4):
                            slot = (ktp * 4 + m) * 1024
                            wv = [sc_toks[m]]
                            if ktp == 0 and m == 0:
                                wv.append(av_prev if not first else tv_tok)
                            prev_exp = p.act(stage[:, slot:slot + 1024],
                                             psX[m][:, 0:1024], AF.Exp,
                                             bias=ebias_sb[:], scale=0.125,
                                             waits=wv)
                            exp_tok[(ktp, m)] = prev_exp
                    av_last = {}
                    for kt in range(8):
                        for m in range(4):
                            w0 = [exp_tok[(kt // 2, m)]]
                            if kt == 0:
                                w0.append(prev_exp)
                            av_last[m] = p.mm(
                                psX[m][:, 0:512], vaug(kt, heads[m]),
                                stage[:, (kt // 2 * 4 + m) * 1024 + (kt % 2) * 512:
                                       (kt // 2 * 4 + m) * 1024 + (kt % 2) * 512 + 512],
                                start=(kt == 0), stop=False, waits=w0,
                                inc=(kt == 7), sgc=True)
                    prev_exp_b = None
                    for kt in range(8, 16):
                        sc_toks = []
                        for m in range(4):
                            w0 = []
                            if kt == 8 and m == 0:
                                w0 = [prev_exp]
                            elif m == 0 and prev_exp_b is not None:
                                w0 = [prev_exp_b]
                            t = score_mm(m, kt, 1, w0)
                            sc_toks.append(t)
                        for m in range(4):
                            slot = ((kt - 8) // 2 * 4 + m) * 1024 + ((kt - 8) % 2) * 512
                            wv = [sc_toks[m]]
                            if kt == 8:
                                wv.append(av_last[m])
                            e = p.act(stage[:, slot:slot + 512],
                                      psX[m][:, 512:1024], AF.Exp,
                                      bias=ebias_sb[:], scale=0.125, waits=wv)
                            prev_exp_b = e
                            av_last[m] = p.mm(
                                psX[m][:, 0:512], vaug(kt, heads[m]),
                                stage[:, slot:slot + 512],
                                start=False, stop=(kt == 15), waits=[e],
                                inc=(kt == 15), sgc=True)
                    av_prev = av_last[3]
                    for m in range(4):
                        h = heads[m]
                        av_r0 = 0 if h % 2 == 0 else 64
                        dn_r = 64 if h % 2 == 0 else 0
                        t1 = p.dve("reciprocal", recip_sb[:],
                                   psX[m][dn_r:dn_r + 1, 0:512],
                                   waits=[av_last[m], prev_exp_b])
                        t2 = p.mm(psX[m][0:64, 512:1024], ones64f[0:1, :],
                                  recip_sb[:], start=True, stop=True,
                                  waits=[t1], inc=True)
                        t3 = p.dve("tensor_copy", rb_sb[:],
                                   psX[m][0:64, 512:1024], waits=[t2])
                        av_written = p.dve(
                            "tensor_mul",
                            av_sb[64 * (h % 2):64 * (h % 2) + 64,
                                  (h // 2) * SJ:(h // 2 + 1) * SJ],
                            psX[m][av_r0:av_r0 + 64, 0:512], rb_sb[:],
                            waits=[t3, av_free])
                        norm_done[m] = av_written
                for ec in range(NDC):
                    sl = ec % 2
                    ps = psX[sl][:, 0:512]
                    for dc in range(NDC):
                        w0 = []
                        if dc == 0:
                            w0 = [tw_out, av_written, outproj_act[sl],
                                  norm_done[sl]]
                        t = p.mm(ps, wbuf[:, dc * D + ec * 128:dc * D + (ec + 1) * 128],
                                 av_sb[:, dc * SJ:(dc + 1) * SJ],
                                 start=(dc == 0), stop=(dc == NDC - 1),
                                 waits=w0, inc=(dc == NDC - 1))
                    ta_ = p.act(tmp_f[sl][:], ps, AF.Identity,
                                bias=bout_sb[:, ec:ec + 1],
                                waits=[t, t_bout, outproj_dve[sl]])
                    outproj_act[sl] = ta_
                    tx = p.dma("sync", x1t[sl][:],
                               x_d[b, ec * 128:(ec + 1) * 128, j * SJ:(j + 1) * SJ],
                               sem="sx%d" % sl,
                               waits=[outproj_st[sl], outproj_ld[sl]])
                    outproj_ld[sl] = tx
                    td = p.dve("tensor_add", x1t[sl][:], x1t[sl][:],
                               tmp_f[sl][:], waits=[ta_, tx])
                    outproj_dve[sl] = td
                    outproj_st[sl] = p.dma(
                        "gpsimd",
                        x1_d[b, ec * 128:(ec + 1) * 128, j * SJ:(j + 1) * SJ],
                        x1t[sl][:], sem="stx%d" % sl, waits=[td])
                av_free = t
            stage_free = av_prev
            qk_free = av_prev
            v_free = av_prev
            w_free = t

        p.emit("sync", lambda e: e.nop(),
               waits=[(name, cnt) for name, cnt in p.counts.items() if cnt])
        with nc.Block() as block:
            p.replay(block)
    return nc


def build_l0_ffn(npos=POS, nt=T):
    nc = bass.Bass("TRN2", target_bir_lowering=False)
    x_d = nc.dram_tensor("x1s", [nt, D, npos], F32, kind="ExternalInput").ap()
    wg_d = {}
    for nm in ("wfT", "ufT", "wiT", "uiT", "wcT", "ucT", "woT", "uoT"):
        wg_d[nm] = nc.dram_tensor(nm, [D, D], F32, kind="ExternalInput").ap()
    bias_d = {}
    for nm in ("bf_rs", "bi_rs", "bc_rs", "bo_rs", "inc_rs"):
        bias_d[nm] = nc.dram_tensor(nm, [128, NDC], F32, kind="ExternalInput").ap()
    x2_d = nc.dram_tensor("x2", [nt, D, npos], F32, kind="ExternalOutput").ap()
    y_hbm = nc.dram_tensor("y2_hbm", [nt, D, npos], BF16, kind="Internal")

    NP8 = NDC * npos
    with ExitStack() as ctx:
        wsb = {nm: sbt(nc, ctx, nm, [128, 8 * D], BF16)
               for nm in ("wf", "wi", "wc", "uf", "ui", "uc")}
        c_sb = [sbt(nc, ctx, f"c_sb{i}", [128, NP8], F32) for i in range(2)]
        cbf = [sbt(nc, ctx, f"cbf{i}", [128, NP8], BF16) for i in range(2)]
        fga = {g: sbt(nc, ctx, f"g_{g}", [128, NP8], F32) for g in "fig"}
        ybuf = [sbt(nc, ctx, f"ybuf{i}", [128, NP8], BF16) for i in range(2)]
        tmp2 = [sbt(nc, ctx, f"scnt{i}", [128, npos], F32) for i in range(2)]
        cuo_sb = sbt(nc, ctx, "cuo_sb", [128, NP8], F32)
        ones_t = sbt(nc, ctx, "ones_t", [128, npos], F32)
        bsb = {nm: sbt(nc, ctx, "sb_" + nm, [128, NDC], F32)
               for nm in ("bf", "bi", "bc", "bo", "inc")}
        hbufs = [sbt(nc, ctx, f"hbuf{i}", [128, npos], F32) for i in range(2)]
        xfb = [sbt(nc, ctx, f"xfb{i}", [128, npos], F32) for i in range(2)]
        x2b = [sbt(nc, ctx, f"x2b{i}", [128, npos], F32) for i in range(2)]
        yl = [sbt(nc, ctx, f"yl{i}", [128, NP8], BF16) for i in range(2)]

        psX = [pst(nc, ctx, f"ps{i}", [128, 1024]) for i in range(4)]
        slots = [psX[0][:, 0:npos], psX[0][:, 512:512 + npos],
                 psX[1][:, 0:npos], psX[1][:, 512:512 + npos],
                 psX[2][:, 512:512 + npos], psX[3][:, 512:512 + npos]]
        NSL = len(slots)
        p = Plan(nc, ctx)
        ln = make_ln(nc, ctx, p, npos, ps_stat=psX[2][0:64, 0:npos],
                     ps_bc=psX[3][:, 0:2 * npos])

        p.pool("memset", ones_t[:], 1.0)
        t_cst = None
        for nm, d in (("bf", "bf_rs"), ("bi", "bi_rs"), ("bc", "bc_rs"),
                      ("bo", "bo_rs"), ("inc", "inc_rs")):
            t_cst = p.dma("gpsimd", bsb[nm][:], bias_d[d][:, :], sem="cst")
        t_w = None
        for nm, d in (("wf", "wfT"), ("wi", "wiT"), ("wc", "wcT"),
                      ("uf", "ufT"), ("ui", "uiT"), ("uc", "ucT")):
            for dc in range(NDC):
                t_w = p.dma("gpsimd", wsb[nm][:, dc * D:(dc + 1) * D],
                            wg_d[d][dc * 128:(dc + 1) * 128, :], sem="wld")

        t_cbf = None
        for mc in range(NDC):
            p.dve("tensor_scalar_mul", c_sb[0][:, mc * npos:(mc + 1) * npos],
                  ones_t[:, :], bsb["inc"][:, mc:mc + 1],
                  waits=[t_cst, ("pl", p.counts["pl"])])
            t_cbf = p.dve("tensor_copy", cbf[0][:, mc * npos:(mc + 1) * npos],
                          c_sb[0][:, mc * npos:(mc + 1) * npos])

        gate_bias = {"f": "bf", "i": "bi", "c": "bc"}
        gate_w = {"f": ("wf", "uf"), "i": ("wi", "ui"), "c": ("wc", "uc")}
        gate_buf = {"f": "f", "i": "i", "c": "g"}

        ln_prev = None
        y_free = [None, None]
        y_str = [None, None]
        upd_tok = [None] * NDC
        slot_act = [None] * 8
        nps = 0
        gmm_last = None
        for t in range(nt):
            sl = t % 2
            def xrows(dc, t=t):
                return x_d[t, dc * 128:(dc + 1) * 128, :]
            def yv(dc, sl=sl):
                return ybuf[sl][:, dc * npos:(dc + 1) * npos]
            ln_prev = ln(xrows, yv, wait_loads=[y_free[sl], y_str[sl]],
                         prev=ln_prev)
            y_tok = ln_prev["apply"]
            ydst = y_hbm.ap()[t].rearrange("(a b) c -> b a c", b=128)
            y_str[sl] = p.dma("gpsimd", ydst, ybuf[sl][:],
                              sem="yst%d" % sl, waits=[y_tok])
            cp_ = t % 2
            for mc in range(NDC):
                sig_toks = {}
                for g in "fic":
                    wn, un = gate_w[g]
                    q = nps % NSL
                    ps = slots[q]
                    nps += 1
                    for dc in range(NDC):
                        w0 = []
                        if dc == 0:
                            w0 = [t_w, t_cbf, y_tok, slot_act[q]]
                        p.mm(ps, wsb[un][:, dc * D + mc * 128:dc * D + (mc + 1) * 128],
                             cbf[cp_][:, dc * npos:(dc + 1) * npos],
                             start=(dc == 0), stop=False, waits=w0, sgc=True)
                    for dc in range(NDC):
                        gmm_last = p.mm(
                            ps, wsb[wn][:, dc * D + mc * 128:dc * D + (mc + 1) * 128],
                            ybuf[sl][:, dc * npos:(dc + 1) * npos],
                            start=False, stop=(dc == NDC - 1), sgc=True,
                            inc=(dc == NDC - 1))
                    fn_act = AF.Sigmoid if g in "fi" else AF.Tanh
                    sa = p.act(fga[gate_buf[g]][:, mc * npos:(mc + 1) * npos],
                               ps, fn_act, bias=bsb[gate_bias[g]][:, mc:mc + 1],
                               waits=[gmm_last, upd_tok[mc]])
                    slot_act[q] = sa
                    sig_toks[g] = sa
                cold = c_sb[cp_][:, mc * npos:(mc + 1) * npos]
                cnew = c_sb[1 - cp_][:, mc * npos:(mc + 1) * npos]
                p.dve("tensor_mul", tmp2[0][:], fga["f"][:, mc * npos:(mc + 1) * npos],
                      cold, waits=[sig_toks["f"]])
                p.dve("tensor_mul", tmp2[1][:], fga["i"][:, mc * npos:(mc + 1) * npos],
                      fga["g"][:, mc * npos:(mc + 1) * npos],
                      waits=[sig_toks["i"], sig_toks["c"]])
                p.dve("tensor_add", cnew, tmp2[0][:], tmp2[1][:])
                t_cbf = p.dve("tensor_copy",
                              cbf[1 - cp_][:, mc * npos:(mc + 1) * npos], cnew)
                upd_tok[mc] = t_cbf
            y_free[sl] = gmm_last

        t_w2 = None
        for nm, d in (("wf", "woT"), ("wi", "uoT")):
            for dc in range(NDC):
                t_w2 = p.dma("gpsimd", wsb[nm][:, dc * D:(dc + 1) * D],
                             wg_d[d][dc * 128:(dc + 1) * 128, :], sem="wld",
                             waits=[gmm_last])
        for mc in range(NDC):
            q = nps % NSL
            ps = slots[q]
            nps += 1
            for dc in range(NDC):
                w0 = []
                if dc == 0:
                    w0 = [t_w2, t_cbf, slot_act[q]]
                tmm = p.mm(ps, wsb["wi"][:, dc * D + mc * 128:dc * D + (mc + 1) * 128],
                           cbf[nt % 2][:, dc * npos:(dc + 1) * npos],
                           start=(dc == 0), stop=(dc == NDC - 1), waits=w0,
                           inc=(dc == NDC - 1))
            te = p.dve("tensor_copy", cuo_sb[:, mc * npos:(mc + 1) * npos], ps,
                       waits=[tmm])
            slot_act[q] = te
        cuo_tok = te
        yl_free = [None, None]
        h_free = [None, None]
        x2_st = [None, None]
        x2_ld = [None, None]
        hadd = [None, None]
        for t in range(nt):
            sl = t % 2
            ysrc = y_hbm.ap()[t].rearrange("(a b) c -> b a c", b=128)
            tyl = p.dma("gpsimd", yl[sl][:], ysrc, sem="ylb%d" % sl,
                        waits=[yl_free[sl]] +
                              [(nm, p.counts[nm]) for nm in ("yst0", "yst1")
                               if nm in p.counts])
            for mc in range(NDC):
                q = nps % NSL
                ps = slots[q]
                nps += 1
                for dc in range(NDC):
                    w0 = []
                    if dc == 0:
                        w0 = [tyl, cuo_tok, slot_act[q]]
                    tmm = p.mm(ps, wsb["wf"][:, dc * D + mc * 128:dc * D + (mc + 1) * 128],
                               yl[sl][:, dc * npos:(dc + 1) * npos],
                               start=(dc == 0), stop=(dc == NDC - 1), waits=w0,
                               inc=(dc == NDC - 1))
                hsl = mc % 2
                tadd = p.dve("tensor_add", ps, ps,
                             cuo_sb[:, mc * npos:(mc + 1) * npos],
                             waits=[tmm, hadd[hsl]])
                th = p.act(hbufs[hsl][:], ps, AF.Tanh,
                           bias=bsb["bo"][:, mc:mc + 1], waits=[tadd, hadd[hsl]])
                slot_act[q] = th
                tx = p.dma("sync", xfb[hsl][:],
                           x_d[t, mc * 128:(mc + 1) * 128, :],
                           sem="hx%d" % hsl, waits=[x2_ld[hsl], hadd[hsl]])
                x2_ld[hsl] = tx
                ta = p.dve("tensor_add", x2b[hsl][:], xfb[hsl][:], hbufs[hsl][:],
                           waits=[th, tx, x2_st[hsl]])
                hadd[hsl] = ta
                x2_st[hsl] = p.dma("gpsimd",
                                   x2_d[t, mc * 128:(mc + 1) * 128, :],
                                   x2b[hsl][:], sem="hs%d" % hsl, waits=[ta])
            yl_free[sl] = tmm

        p.emit("sync", lambda e: e.nop(),
               waits=[(name, cnt) for name, cnt in p.counts.items() if cnt])
        with nc.Block() as block:
            p.replay(block)
    return nc


def prep_l0_ffn(inputs):
    return {
        "wfT": np.ascontiguousarray(inputs["L0_ffn_wf"].T),
        "ufT": np.ascontiguousarray(inputs["L0_ffn_uf"].T),
        "wiT": np.ascontiguousarray(inputs["L0_ffn_wi"].T),
        "uiT": np.ascontiguousarray(inputs["L0_ffn_ui"].T),
        "wcT": np.ascontiguousarray(inputs["L0_ffn_wc"].T),
        "ucT": np.ascontiguousarray(inputs["L0_ffn_uc"].T),
        "woT": np.ascontiguousarray(inputs["L0_ffn_wo"].T),
        "uoT": np.ascontiguousarray(inputs["L0_ffn_uo"].T),
        "bf_rs": _rs(inputs["L0_ffn_bf"]), "bi_rs": _rs(inputs["L0_ffn_bi"]),
        "bc_rs": _rs(inputs["L0_ffn_bc"]), "bo_rs": _rs(inputs["L0_ffn_bo"]),
        "inc_rs": _rs(inputs["L0_ffn_in_c"]),
    }


def run_l0_ffn(inputs, X1f):
    nc = build_l0_ffn()
    common = prep_l0_ffn(inputs)
    in_maps = []
    for i in range(NCORES):
        m = dict(common)
        m["x1s"] = np.ascontiguousarray(X1f[:, :, POS * i:POS * (i + 1)])
        in_maps.append(m)
    res = run_bass_kernel_spmd(nc, in_maps, core_ids=list(range(NCORES)))
    out = np.empty((T, D, S), np.float32)
    for i in range(NCORES):
        out[:, :, POS * i:POS * (i + 1)] = res.results[i]["x2"]
    return out


def build_l1_ffn(nb=B_PER):
    nc = bass.Bass("TRN2", target_bir_lowering=False)
    x_d = nc.dram_tensor("x3", [nb, D, S], F32, kind="ExternalInput").ap()
    wr_d = nc.dram_tensor("WrT", [D, D], F32, kind="ExternalInput").ap()
    wk_d = nc.dram_tensor("WkT", [D, D], F32, kind="ExternalInput").ap()
    wv_d = nc.dram_tensor("WvT", [D, D], F32, kind="ExternalInput").ap()
    mur_d = nc.dram_tensor("mur_rs", [128, NDC], F32, kind="ExternalInput").ap()
    muk_d = nc.dram_tensor("muk_rs", [128, NDC], F32, kind="ExternalInput").ap()
    x4_d = nc.dram_tensor("x4", [nb, D, S], F32, kind="ExternalOutput").ap()

    with ExitStack() as ctx:
        wsb = {nm: sbt(nc, ctx, "w_" + nm, [128, 8 * D], BF16)
               for nm in ("wr", "wk", "wv")}
        y_sb = sbt(nc, ctx, "y_sb", [128, NDC * S], BF16)
        xr_sb = sbt(nc, ctx, "xr_sb", [128, NDC * SJ], BF16)
        xk_sb = sbt(nc, ctx, "xk_sb", [128, NDC * SJ], BF16)
        dsc = sbt(nc, ctx, "dsc", [128, SJ], BF16)
        r_sb = sbt(nc, ctx, "r_sb", [128, NDC * SJ], BF16)
        kact = sbt(nc, ctx, "kact", [128, NDC * SJ], BF16)
        krel = sbt(nc, ctx, "krel", [128, SJ], BF16)
        musb = {"r": sbt(nc, ctx, "mur_sb", [128, NDC], F32),
                "k": sbt(nc, ctx, "muk_sb", [128, NDC], F32)}
        tmpv = [sbt(nc, ctx, f"tmpv{i}", [128, SJ], F32) for i in range(2)]
        xfb = [sbt(nc, ctx, f"xfb{i}", [128, SJ], F32) for i in range(2)]
        x4b = [sbt(nc, ctx, f"x4b{i}", [128, SJ], F32) for i in range(2)]

        psX = [pst(nc, ctx, f"ps{i}", [128, 1024]) for i in range(4)]
        slots = [psX[0][:, 0:512], psX[0][:, 512:1024],
                 psX[1][:, 0:512], psX[1][:, 512:1024],
                 psX[2][:, 512:1024]]
        NSL = len(slots)
        p = Plan(nc, ctx)
        ln = make_ln(nc, ctx, p, SJ, ps_stat=psX[2][0:64, 0:512],
                     ps_bc=psX[3][:, 0:1024])

        t_cst = p.dma("gpsimd", musb["r"][:], mur_d[:, :], sem="cst")
        t_cst = p.dma("gpsimd", musb["k"][:], muk_d[:, :], sem="cst")
        t_w = None
        for nm, d in (("wr", wr_d), ("wk", wk_d), ("wv", wv_d)):
            for dc in range(NDC):
                t_w = p.dma("gpsimd", wsb[nm][:, dc * D:(dc + 1) * D],
                            d[dc * 128:(dc + 1) * 128, :], sem="wld")

        ln_prev = None
        slot_rd = [None] * NSL
        xmix_free = None
        r_free = None
        kact_free = None
        vres = [None, None]
        vres_act = [None, None]
        x4_st = [None, None]
        x4_ld = [None, None]
        y_stats_free = None
        for b in range(nb):
            for j in range(NJ):
                def xrows(dc, b=b, j=j):
                    return x_d[b, dc * 128:(dc + 1) * 128, j * SJ:(j + 1) * SJ]
                def yv(dc, b=b, j=j):
                    return y_sb[:, dc * S + j * SJ:dc * S + (j + 1) * SJ]
                ln_prev = ln(xrows, yv, wait_loads=[y_stats_free], prev=ln_prev)
            y_tok = ln_prev["apply"]
            for j in range(NJ):
                tmix = None
                for dc in range(NDC):
                    yj = y_sb[:, dc * S + j * SJ:dc * S + (j + 1) * SJ]
                    for side, dst in (("r", xr_sb), ("k", xk_sb)):
                        mu = musb[side][:, dc:dc + 1]
                        o = dc * SJ
                        if j == 0:
                            yp = y_sb[:, dc * S + j * SJ:dc * S + (j + 1) * SJ - 1]
                            p.dve("tensor_sub", dsc[:, 1:SJ],
                                  y_sb[:, dc * S + 1:dc * S + SJ], yp,
                                  waits=[y_tok, xmix_free])
                            p.dve("scalar_tensor_tensor", dst[:, o + 1:o + SJ],
                                  dsc[:, 1:SJ], mu, yp, AL.mult, AL.add,
                                  waits=[t_cst])
                            tmix = p.dve("tensor_scalar_mul", dst[:, o:o + 1],
                                         yj[:, 0:1], mu)
                        else:
                            yp = y_sb[:, dc * S + j * SJ - 1:dc * S + (j + 1) * SJ - 1]
                            p.dve("tensor_sub", dsc[:], yj, yp,
                                  waits=[y_tok, xmix_free])
                            tmix = p.dve("scalar_tensor_tensor", dst[:, o:o + SJ],
                                         dsc[:], mu, yp, AL.mult, AL.add,
                                         waits=[t_cst])
                nps = 0
                rk_tok = {}
                for side, src, dst in (("r", xr_sb, r_sb), ("k", xk_sb, kact)):
                    wn = "wr" if side == "r" else "wk"
                    for oc in range(NDC):
                        q = nps % NSL
                        ps = slots[q]
                        nps += 1
                        for dc in range(NDC):
                            w0 = []
                            if dc == 0:
                                w0 = [t_w, tmix, slot_rd[q]]
                            t = p.mm(ps, wsb[wn][:, dc * D + oc * 128:dc * D + (oc + 1) * 128],
                                     src[:, dc * SJ:(dc + 1) * SJ],
                                     start=(dc == 0), stop=(dc == NDC - 1),
                                     waits=w0, inc=(dc == NDC - 1))
                        if side == "r":
                            tk = p.act(dst[:, oc * SJ:(oc + 1) * SJ], ps,
                                       AF.Sigmoid, waits=[t, r_free])
                        else:
                            p.act(krel[:], ps, AF.Relu, waits=[t])
                            tk = p.act(dst[:, oc * SJ:(oc + 1) * SJ], krel[:],
                                       AF.Square, waits=[kact_free])
                        slot_rd[q] = tk
                        rk_tok[(side, oc)] = tk
                xmix_free = t
                for o2 in range(NDC):
                    q = nps % NSL
                    ps = slots[q]
                    nps += 1
                    for dc in range(NDC):
                        w0 = []
                        if dc == 0:
                            w0 = [rk_tok[("k", NDC - 1)], slot_rd[q]]
                        t = p.mm(ps, wsb["wv"][:, dc * D + o2 * 128:dc * D + (o2 + 1) * 128],
                                 kact[:, dc * SJ:(dc + 1) * SJ],
                                 start=(dc == 0), stop=(dc == NDC - 1),
                                 waits=w0, inc=(dc == NDC - 1))
                    kact_free = t
                    sl = o2 % 2
                    tv = p.dve("tensor_mul", tmpv[sl][:], ps,
                               r_sb[:, o2 * SJ:(o2 + 1) * SJ],
                               waits=[t, rk_tok[("r", o2)], vres[sl]])
                    slot_rd[q] = tv
                    r_free = tv
                    tx = p.dma("sync", xfb[sl][:],
                               x_d[b, o2 * 128:(o2 + 1) * 128, j * SJ:(j + 1) * SJ],
                               sem="sx%d" % sl, waits=[x4_ld[sl], vres[sl]])
                    x4_ld[sl] = tx
                    ta = p.dve("tensor_add", x4b[sl][:], xfb[sl][:], tmpv[sl][:],
                               waits=[tx, x4_st[sl]])
                    vres[sl] = ta
                    x4_st[sl] = p.dma("gpsimd",
                                      x4_d[b, o2 * 128:(o2 + 1) * 128,
                                           j * SJ:(j + 1) * SJ],
                                      x4b[sl][:], sem="stx%d" % sl, waits=[ta])
            y_stats_free = xmix_free

        p.emit("sync", lambda e: e.nop(),
               waits=[(name, cnt) for name, cnt in p.counts.items() if cnt])
        with nc.Block() as block:
            p.replay(block)
    return nc


def prep_l1_ffn(inputs):
    return {"WrT": np.ascontiguousarray(inputs["L1_ffn_W_r"].T),
            "WkT": np.ascontiguousarray(inputs["L1_ffn_W_k"].T),
            "WvT": np.ascontiguousarray(inputs["L1_ffn_W_v"].T),
            "mur_rs": _rs(inputs["L1_ffn_mu_r"]),
            "muk_rs": _rs(inputs["L1_ffn_mu_k"])}


def run_l1_ffn(inputs, X3f):
    nc = build_l1_ffn()
    common = prep_l1_ffn(inputs)
    in_maps = []
    for i in range(NCORES):
        m = dict(common)
        m["x3"] = np.ascontiguousarray(X3f[B_PER * i:B_PER * (i + 1)])
        in_maps.append(m)
    res = run_bass_kernel_spmd(nc, in_maps, core_ids=list(range(NCORES)))
    return np.concatenate([r["x4"] for r in res.results], axis=0)


def _perm_qk():
    perm = np.zeros(D, dtype=np.int64)
    for n in range(512):
        perm[n] = 64 * (n // 32) + 2 * (n % 32)
        perm[512 + n] = 64 * (n // 32) + 2 * (n % 32) + 1
    return perm


def _rope_tables_rep():
    inv = (np.float32(1.0) / (np.float32(10000.0) **
           (np.arange(0, HD, 2, dtype=np.float32) / np.float32(HD)))).astype(np.float32)
    t = np.arange(S, dtype=np.float32)
    fr = np.outer(t, inv).astype(np.float32)
    cos = np.cos(fr).astype(np.float32).T
    sin = np.sin(fr).astype(np.float32).T
    return (np.ascontiguousarray(np.tile(cos, (4, 1))),
            np.ascontiguousarray(np.tile(sin, (4, 1))))


def _rs(v):
    return np.ascontiguousarray(v.reshape(NDC, 128).T)


def prep_l0_attn(inputs):
    wT = np.ascontiguousarray(inputs["L0_attn_w_qkv"].T)
    perm = _perm_qk()
    wqkvT = np.ascontiguousarray(np.concatenate(
        [wT[:, 0:D][:, perm], wT[:, D:2 * D][:, perm], wT[:, 2 * D:3 * D]],
        axis=1))
    cos_rep, sin_rep = _rope_tables_rep()
    return {"wqkvT": wqkvT,
            "woutT": np.ascontiguousarray(inputs["L0_attn_w_out"].T),
            "bout_rs": _rs(inputs["L0_attn_b_out"]),
            "cos_rep": cos_rep, "sin_rep": sin_rep}


def prep_l1_attn(inputs):
    return {"Wq": inputs["L1_attn_Wq"], "Wk": inputs["L1_attn_Wk"],
            "Wv": inputs["L1_attn_Wv"], "woutT": inputs["L1_attn_Wo"],
            "bout_rs": _rs(inputs["L1_attn_bo"])}


def run_attn(inputs, Xf, rope):
    nc = build_attn_layer(rope=rope)
    common = prep_l0_attn(inputs) if rope else prep_l1_attn(inputs)
    in_maps = []
    for i in range(NCORES):
        m = dict(common)
        m["x"] = np.ascontiguousarray(Xf[B_PER * i:B_PER * (i + 1)])
        in_maps.append(m)
    res = run_bass_kernel_spmd(nc, in_maps, core_ids=list(range(NCORES)))
    return np.concatenate([r["x1"] for r in res.results], axis=0)


def kernel(**inputs):
    Xf = np.ascontiguousarray(inputs["L0_attn_x"].transpose(0, 2, 1))
    X1f = run_attn(inputs, Xf, rope=True)
    X2f = run_l0_ffn(inputs, X1f)
    X3f = run_attn(inputs, X2f, rope=False)
    X4f = run_l1_ffn(inputs, X3f)
    return np.ascontiguousarray(X4f.transpose(0, 2, 1))
